# version 05
# baseline (speedup 1.0000x reference)
from contextlib import ExitStack

import numpy as np
import ml_dtypes
import concourse.bass as bass
import concourse.mybir as mybir
from concourse.bass_utils import run_bass_kernel_spmd

F32 = mybir.dt.float32
BF16 = mybir.dt.bfloat16
ACT = mybir.ActivationFunctionType
ALU = mybir.AluOpType
AX = mybir.AxisListType

D = 2048
NT = 16
NF = 4
B = 8
HSL = 1024
NHC = HSL // 128
KSTEPS = 4

BFD = ml_dtypes.bfloat16


class Sem:

    def __init__(self, nc, name):
        self.h = nc.alloc_semaphore(name)
        self.n = 0

    def inc(self, inst, k=1):
        inst.then_inc(self.h, k)
        self.n += k
        return self.n

    def wait(self, eng, val=None):
        eng.wait_ge(self.h, self.n if val is None else val)



def _dump_and_stop(nc, sbuf_list, dram_list=()):
    sdd = Sem(nc, "sdd")
    with nc.Block() as blk:
        @blk.sync
        def _(sy):
            for name, t in sbuf_list:
                hb = nc.dram_tensor("dbg_" + name, list(t.shape), t.dtype,
                                    kind="ExternalOutput")
                sdd.inc(sy.dma_start(out=hb[:, :], in_=t[:, :]), 16)
            for name, t in dram_list:
                hb = nc.dram_tensor("dbg_" + name, list(t.shape), t.dtype,
                                    kind="ExternalOutput")
                ap_in = t.ap() if hasattr(t, "ap") else t
                sdd.inc(sy.dma_start(out=hb.ap(), in_=t.ap()), 16)
            sdd.wait(sy)
    return nc


def build_nc(stop_after=None):
    nc = bass.Bass("TRN2", target_bir_lowering=False)

    x_h = nc.dram_tensor("x", [D, D], F32, kind="ExternalInput")
    wq_h = nc.dram_tensor("wq", [D, D], F32, kind="ExternalInput")
    wk_h = nc.dram_tensor("wk", [D, D], F32, kind="ExternalInput")
    wo_h = nc.dram_tensor("wo", [D, D], F32, kind="ExternalInput")
    gatew_h = nc.dram_tensor("gatew", [D, D], F32, kind="ExternalInput")
    aggw_h = nc.dram_tensor("aggw_sl", [D, 256], F32, kind="ExternalInput")
    bcw_h = nc.dram_tensor("bcw_sl", [D, 256], F32, kind="ExternalInput")
    wv_h = nc.dram_tensor("wv_blk", [NT, D, 128], F32, kind="ExternalInput")
    d1s_h = nc.dram_tensor("d1s", [D, HSL], F32, kind="ExternalInput")
    d2s_h = nc.dram_tensor("d2s", [HSL, D], F32, kind="ExternalInput")
    lastrows_h = nc.dram_tensor("lastrows", [B, D], F32, kind="ExternalInput")
    in_g8_h = nc.dram_tensor("in_g8", [B, D], F32, kind="ExternalInput")
    in_b8_h = nc.dram_tensor("in_b8", [B, D], F32, kind="ExternalInput")
    pn_g8_h = nc.dram_tensor("pn_g8", [B, D], F32, kind="ExternalInput")
    pn_b8_h = nc.dram_tensor("pn_b8", [B, D], F32, kind="ExternalInput")
    agg_b8_h = nc.dram_tensor("agg_b8", [B, D], F32, kind="ExternalInput")
    d1_b8_h = nc.dram_tensor("d1_b8", [B, HSL], F32, kind="ExternalInput")
    d2_b8_h = nc.dram_tensor("d2_b8", [B, D], F32, kind="ExternalInput")
    bc_b8_h = nc.dram_tensor("bc_b8", [B, D], F32, kind="ExternalInput")
    og128_h = nc.dram_tensor("og128", [128, D], F32, kind="ExternalInput")
    ob128_h = nc.dram_tensor("ob128", [128, D], F32, kind="ExternalInput")
    gate_b_h = nc.dram_tensor("gate_b_row", [1, D], F32, kind="ExternalInput")
    sel_h = nc.dram_tensor("sel_rep", [B, 128], F32, kind="ExternalInput")
    ident_h = nc.dram_tensor("ident", [128, 128], BF16, kind="ExternalInput")
    onescol_h = nc.dram_tensor("ones_col", [128, 1], BF16, kind="ExternalInput")
    ones1_h = nc.dram_tensor("ones1", [1, 128], BF16, kind="ExternalInput")

    out_h = nc.dram_tensor("out", [D, D], F32, kind="ExternalOutput")

    gate_sp = nc.dram_tensor("gate_sp", [D, D], BF16)
    wT_sp = nc.dram_tensor("wT_sp", [128, NT, D], BF16)
    attn_sp = nc.dram_tensor("attn_sp", [128, NT, D], BF16)
    cc_in = nc.dram_tensor("cc_in", [B, D], F32)
    cc_out = nc.dram_tensor("cc_out", [B, D], F32)
    ag_in = nc.dram_tensor("ag_in", [B, 256], F32)
    ag_out = nc.dram_tensor("ag_out", [B * B, 256], F32)

    ident = nc.alloc_sbuf_tensor("identS", [128, 128], BF16)
    ones_col = nc.alloc_sbuf_tensor("ones_colS", [128, 1], BF16)
    ones1 = nc.alloc_sbuf_tensor("ones1S", [1, 128], BF16)
    sel = nc.alloc_sbuf_tensor("selS", [B, 128], F32)
    gate_b_bf = nc.alloc_sbuf_tensor("gate_b_bfS", [1, D], BF16)
    mu_cols = nc.alloc_sbuf_tensor("mu_cols", [128, NT], F32)
    rsig_cols = nc.alloc_sbuf_tensor("rsig_cols", [128, NT], F32)
    r_cols = nc.alloc_sbuf_tensor("r_cols", [128, NT], F32)
    bc_rep = nc.alloc_sbuf_tensor("bc_rep", [128, D], F32)

    sv = Sem(nc, "sv")
    st = Sem(nc, "st")
    ss = Sem(nc, "ss")
    scc = Sem(nc, "scc")

    sdT = Sem(nc, "sdT")
    swW = [Sem(nc, "swW0"), Sem(nc, "swW1")]
    swC = Sem(nc, "swC")
    swD1 = Sem(nc, "swD1")
    swD2 = Sem(nc, "swD2")
    with ExitStack() as _ctx:
        _e = _ctx.enter_context
        t_last = _e(nc.sbuf_tensor("t_last", [B, D], F32))
        t_ing = _e(nc.sbuf_tensor("t_ing", [B, D], F32))
        t_inb = _e(nc.sbuf_tensor("t_inb", [B, D], F32))
        t_png = _e(nc.sbuf_tensor("t_png", [B, D], F32))
        t_pnb = _e(nc.sbuf_tensor("t_pnb", [B, D], F32))
        t_aggb = _e(nc.sbuf_tensor("t_aggb", [B, D], F32))
        t_d1b = _e(nc.sbuf_tensor("t_d1b", [B, HSL], F32))
        t_d2b = _e(nc.sbuf_tensor("t_d2b", [B, D], F32))
        t_bcb = _e(nc.sbuf_tensor("t_bcb", [B, D], F32))
        twork = _e(nc.sbuf_tensor("t_work", [B, D], F32))
        thought = _e(nc.sbuf_tensor("t_thought", [B, D], F32))
        t2w = _e(nc.sbuf_tensor("t_t2", [B, HSL], F32))
        tbf = _e(nc.sbuf_tensor("t_bf", [B, D], BF16))
        tfm = _e(nc.sbuf_tensor("t_fm", [128, NT * B], BF16))
        tstats = _e(nc.sbuf_tensor("t_stats", [B, 26], F32))
        wck0 = _e(nc.sbuf_tensor("t_wck0", [128, D], BF16))
        wck1 = _e(nc.sbuf_tensor("t_wck1", [128, D], BF16))
        d1c = _e(nc.sbuf_tensor("t_d1c", [128, NT * HSL], BF16))
        d2c = _e(nc.sbuf_tensor("t_d2c", [128, NHC * D], BF16))
        ccst = _e(nc.sbuf_tensor("t_ccst", [B, D], F32))
        ptrT = [_e(nc.psum_tensor("t_ptr%d" % i, [128, B], BF16))
                for i in range(3)]
        _mmctx = ExitStack()
        pbig = _mmctx.enter_context(nc.psum_tensor("t_pbig", [B, D], F32))
        blk = _e(nc.Block())
        wck = [wck0, wck1]

        @blk.gpsimd
        def _(g):
            for c in range(NT):
                swD1.inc(g.dma_start(
                    out=d1c[:, c * HSL:(c + 1) * HSL],
                    in_=d1s_h[128 * c:128 * (c + 1), :]), 16)
            for c in range(NHC):
                swD2.inc(g.dma_start(
                    out=d2c[:, c * D:(c + 1) * D],
                    in_=d2s_h[128 * c:128 * (c + 1), :]), 16)
        d1c_ld = swD1.n
        d2c_ld = swD2.n

        @blk.sync
        def _(sy):
            for dst, src in ((t_last, lastrows_h), (t_ing, in_g8_h),
                             (t_inb, in_b8_h), (t_png, pn_g8_h),
                             (t_pnb, pn_b8_h), (t_aggb, agg_b8_h),
                             (t_d2b, d2_b8_h), (t_bcb, bc_b8_h)):
                sdT.inc(sy.dma_start(out=dst[:, :], in_=src[:, :]), 16)
            sdT.inc(sy.dma_start(out=t_d1b[:, :], in_=d1_b8_h[:, :]), 16)
            sdT.inc(sy.dma_start(out=ident[:, :], in_=ident_h[:, :]), 16)
            sdT.inc(sy.dma_start(out=ones_col[:, :], in_=onescol_h[:, :]), 16)
            sdT.inc(sy.dma_start(out=ones1[:, :], in_=ones1_h[:, :]), 16)
            sdT.inc(sy.dma_start(out=sel[:, :], in_=sel_h[:, :]), 16)
        small_loads = sdT.n

        def ln8(src, dst, W, eps, gamma=None, beta=None, pre_wait=None):
            nchunk = (W + 511) // 512

            @blk.vector
            def _(v):
                if pre_wait is not None:
                    pre_wait(v)
                sv.wait(v)
                for c in range(nchunk):
                    ist = v.bn_stats(tstats[:, 6 * c:6 * c + 6],
                                     src[:, 512 * c:512 * (c + 1)])
                iv = sv.inc(ist)
                v.wait_ge(sv.h, iv)
                iv = sv.inc(v.bn_aggr(tstats[:, 24:26], tstats[:, 0:6 * nchunk]))
                v.wait_ge(sv.h, iv)
                sv.inc(v.tensor_scalar_add(tstats[:, 25:26], tstats[:, 25:26], eps))

            @blk.scalar
            def _(s):
                sv.wait(s)
                isx = ss.inc(s.activation(tstats[:, 25:26], tstats[:, 25:26], ACT.Ln))
                s.wait_ge(ss.h, isx)
                ss.inc(s.activation(tstats[:, 25:26], tstats[:, 25:26], ACT.Exp,
                                    scale=-0.5))

            @blk.vector
            def _(v):
                ss.wait(v)
                iv = sv.inc(v.tensor_scalar(dst, src, tstats[:, 24:25],
                                            tstats[:, 25:26], ALU.subtract,
                                            ALU.mult))
                if gamma is not None:
                    v.wait_ge(sv.h, iv)
                    iv = sv.inc(v.tensor_mul(dst, dst, gamma))
                if beta is not None:
                    v.wait_ge(sv.h, iv)
                    sv.inc(v.tensor_add(dst, dst, beta))

        def to_fm(src, W):
            nchunk = W // 128
            @blk.vector
            def _(v):
                sv.wait(v)
                sv.inc(v.tensor_copy(tbf[:, 0:W], src))
            ev_base = sv.n
            tr_base = st.n

            for g4 in range(0, nchunk, 4):
                @blk.tensor
                def _(t, g4=g4):
                    for c in range(g4, min(nchunk, g4 + 4)):
                        if c == 0:
                            sv.wait(t, ev_base)
                        if c >= 3:
                            sv.wait(t, ev_base + c - 2)
                        st.inc(t.transpose(ptrT[c % 3][:, :],
                                           tbf[:, 128 * c:128 * (c + 1)],
                                           ident[0:B, 0:B]))

                @blk.vector
                def _(v, g4=g4):
                    for c in range(g4, min(nchunk, g4 + 4)):
                        v.wait_ge(st.h, tr_base + c + 1)
                        v.tensor_copy(tfm[:, B * c:B * (c + 1)],
                                      ptrT[c % 3][:, :]).then_inc(sv.h, 1)
                        sv.n += 1

        def mm8(whbm, nk, W, psum):
            nN = (W + 511) // 512
            load_val = {}
            mmv = {}

            st_base = st.n

            def emit_load(ck):
                @blk.gpsimd
                def _(g, ck=ck):
                    st.wait(g, mmv[ck - 2] if ck >= 2 else st_base)
                    swW[ck % 2].inc(
                        g.dma_start(out=wck[ck % 2][:, 0:W],
                                    in_=whbm[128 * ck:128 * (ck + 1), 0:W]), 16)
                load_val[ck] = swW[ck % 2].n

            emit_load(0)
            if nk > 1:
                emit_load(1)
            for ck in range(nk):
                @blk.tensor
                def _(t, ck=ck):
                    if ck == 0:
                        sv.wait(t)
                    swW[ck % 2].wait(t, load_val[ck])
                    for n in range(nN):
                        w1 = min(W, 512 * (n + 1))
                        mi = t.matmul(psum[:, 512 * n:w1],
                                      tfm[:, B * ck:B * (ck + 1)],
                                      wck[ck % 2][:, 512 * n:w1],
                                      start=(ck == 0), stop=(ck == nk - 1))
                    st.inc(mi)
                mmv[ck] = st.n
                if ck + 2 < nk:
                    emit_load(ck + 2)

        def mm8c(cache, nk, W, psum, ld_sem, ld_val):
            nN = (W + 511) // 512

            @blk.tensor
            def _(t):
                sv.wait(t)
                ld_sem.wait(t, ld_val)
                for ck in range(nk):
                    for n in range(nN):
                        mi = t.matmul(psum[:, 512 * n:512 * (n + 1)],
                                      tfm[:, B * ck:B * (ck + 1)],
                                      cache[:, ck * W + 512 * n:
                                            ck * W + 512 * (n + 1)],
                                      start=(ck == 0), stop=(ck == nk - 1))
                st.inc(mi)

        def w_small(v):
            sdT.wait(v, small_loads)
        ln8(t_last[:, :], twork[:, :], D, 1e-5, pre_wait=w_small)
        ln8(twork[:, :], thought[:, :], D, 1e-6, gamma=t_ing[:, :],
            beta=t_inb[:, :])
        to_fm(thought[:, :], D)
        mm8(aggw_h, NT, 256, pbig)

        def allgather_to(dst):
            @blk.vector
            def _(v):
                st.wait(v)
                sv.inc(v.tensor_copy(ccst[:, 0:256], pbig[:, 0:256]))

            @blk.gpsimd
            def _(g):
                sv.wait(g)
                swC.inc(g.dma_start(out=ag_in[:, :], in_=ccst[:, 0:256]), 16)
                swC.wait(g)
                scc.inc(g.collective_compute(
                    "AllGather", ALU.bypass, replica_groups=[list(range(B))],
                    ins=[ag_in.ap().opt()], outs=[ag_out.ap().opt()]))
                scc.wait(g)
                for r in range(B):
                    swC.inc(g.dma_start(out=dst[:, 256 * r:256 * (r + 1)],
                                        in_=ag_out[B * r:B * (r + 1), :]), 16)

        allgather_to(ccst)

        @blk.vector
        def _(v):
            swC.wait(v)
            sv.inc(v.tensor_add(thought[:, :], ccst[:, :], t_aggb[:, :]))

        for k in range(KSTEPS):
            ln8(thought[:, :], twork[:, :], D, 1e-6, gamma=t_png[:, :],
                beta=t_pnb[:, :])
            to_fm(twork[:, :], D)
            mm8c(d1c, NT, HSL, pbig, swD1, d1c_ld)

            @blk.vector
            def _(v):
                st.wait(v)
                sv.inc(v.tensor_add(t2w[:, :], pbig[:, 0:HSL], t_d1b[:, :]))

            @blk.scalar
            def _(s):
                sv.wait(s)
                ss.inc(s.activation(t2w[:, :], t2w[:, :], ACT.Gelu))

            @blk.vector
            def _(v):
                ss.wait(v)
                sv.inc(v.tensor_copy(tbf[:, 0:HSL], t2w[:, :]))

            ev_base = sv.n
            tr_base = st.n
            for g4 in range(0, NHC, 4):
                @blk.tensor
                def _(t, g4=g4):
                    for c in range(g4, min(NHC, g4 + 4)):
                        if c == 0:
                            sv.wait(t, ev_base)
                        if c >= 3:
                            sv.wait(t, ev_base + c - 2)
                        st.inc(t.transpose(ptrT[c % 3][:, :],
                                           tbf[:, 128 * c:128 * (c + 1)],
                                           ident[0:B, 0:B]))

                @blk.vector
                def _(v, g4=g4):
                    for c in range(g4, min(NHC, g4 + 4)):
                        v.wait_ge(st.h, tr_base + c + 1)
                        v.tensor_copy(tfm[:, B * c:B * (c + 1)],
                                      ptrT[c % 3][:, :]).then_inc(sv.h, 1)
                        sv.n += 1

            mm8c(d2c, NHC, D, pbig, swD2, d2c_ld)

            @blk.vector
            def _(v):
                st.wait(v)
                sv.inc(v.tensor_copy(ccst[:, :], pbig[:, :]))

            @blk.gpsimd
            def _(g):
                sv.wait(g)
                swC.inc(g.dma_start(out=cc_in[:, :], in_=ccst[:, :]), 16)
                swC.wait(g)
                scc.inc(g.collective_compute(
                    "AllReduce", ALU.add, replica_groups=[list(range(B))],
                    ins=[cc_in.ap().opt()], outs=[cc_out.ap().opt()]))
                scc.wait(g)
                swC.inc(g.dma_start(out=ccst[:, :], in_=cc_out[:, :]), 16)

            @blk.vector
            def _(v):
                swC.wait(v)
                iv = sv.inc(v.tensor_add(thought[:, :], thought[:, :], ccst[:, :]))
                v.wait_ge(sv.h, iv)
                sv.inc(v.tensor_add(thought[:, :], thought[:, :], t_d2b[:, :]))

        to_fm(thought[:, :], D)
        mm8(bcw_h, NT, 256, pbig)
        allgather_to(twork)

        @blk.vector
        def _(v):
            swC.wait(v)
            sv.inc(v.tensor_add(ccst[:, :], twork[:, :], t_bcb[:, :]))

        _mmctx.close()
        with nc.psum_tensor("t_prep", [128, D], F32) as prep:
            @blk.tensor
            def _(t):
                sv.wait(t)
                sdT.wait(t, small_loads)
                for n in range(NF):
                    st.inc(t.matmul(prep[:, 512 * n:512 * (n + 1)], sel[:, :],
                                    ccst[:, 512 * n:512 * (n + 1)],
                                    start=True, stop=True))

            @blk.vector
            def _(v):
                st.wait(v)
                sv.inc(v.tensor_copy(bc_rep[:, :], prep[:, :]))

    if stop_after == "T":
        return _dump_and_stop(nc, [("bc_rep", bc_rep)])

    sdX = [Sem(nc, "sdX0"), Sem(nc, "sdX1")]
    swS = [Sem(nc, "swS0"), Sem(nc, "swS1")]
    xn_ctx = nc.sbuf_tensor("xn_fm", [128, NT * D], BF16)
    xn_fm = xn_ctx.__enter__()
    q_ctx = nc.sbuf_tensor("q_tm", [128, NT * D], BF16)
    q_tm = q_ctx.__enter__()
    with ExitStack() as _ctx:
        _e = _ctx.enter_context
        xt0 = _e(nc.sbuf_tensor("b1_x0", [128, D], F32))
        xt1 = _e(nc.sbuf_tensor("b1_x1", [128, D], F32))
        xb0 = _e(nc.sbuf_tensor("b1_bf0", [128, D], BF16))
        xb1 = _e(nc.sbuf_tensor("b1_bf1", [128, D], BF16))
        bst = _e(nc.sbuf_tensor("b1_st", [128, 26], F32))
        w0 = _e(nc.sbuf_tensor("b2_w0", [128, NT * 512], BF16))
        w1 = _e(nc.sbuf_tensor("b2_w1", [128, NT * 512], BF16))
        tp0 = _e(nc.psum_tensor("b1_p0", [128, 128], BF16))
        tp1 = _e(nc.psum_tensor("b1_p1", [128, 128], BF16))
        tp2 = _e(nc.psum_tensor("b1_p2", [128, 128], BF16))
        tp3 = _e(nc.psum_tensor("b1_p3", [128, 128], BF16))
        p0 = _e(nc.psum_tensor("b2_p0", [128, 512], F32))
        p1 = _e(nc.psum_tensor("b2_p1", [128, 512], F32))
        blk = _e(nc.Block())
        xts = [xt0, xt1]
        xbs = [xb0, xb1]
        tps = [tp0, tp1, tp2, tp3]
        ws = [w0, w1]
        ps = [p0, p1]
        load_val = {}
        apply_val = {}
        tr_val = {}
        ev_val = {}
        wload = {}
        mmv = {}
        evv = {}

        def wslab_load(fs, whbm):
            @blk.gpsimd
            def _(g, fs=fs):
                if fs >= 2:
                    st.wait(g, mmv[(fs - 2, NT - 1)])
                for c in range(NT):
                    swS[fs % 2].inc(g.dma_start(
                        out=ws[fs % 2][:, c * 512:(c + 1) * 512],
                        in_=whbm[c * 128:(c + 1) * 128,
                                 fs * 512:(fs + 1) * 512]), 16)
            wload[fs] = swS[fs % 2].n

        def emit_load(tt):
            @blk.sync
            def _(sy, tt=tt):
                if tt >= 2:
                    sv.wait(sy, apply_val[tt - 2])
                sdX[tt % 2].inc(sy.dma_start(
                    out=xts[tt % 2][:, :],
                    in_=x_h[128 * tt:128 * (tt + 1), :]), 16)
            load_val[tt] = sdX[tt % 2].n

        def b1_tile(tt):
            @blk.vector
            def _(v, tt=tt):
                sdX[tt % 2].wait(v, load_val[tt])
                if tt >= 2:
                    st.wait(v, tr_val.get((tt - 2, 15), 0))
                sv.wait(v)
                xt = xts[tt % 2]
                for c in range(NF):
                    ist = v.bn_stats(bst[:, 6 * c:6 * c + 6],
                                     xt[:, 512 * c:512 * (c + 1)])
                iv = sv.inc(ist)
                v.wait_ge(sv.h, iv)
                iv = sv.inc(v.bn_aggr(bst[:, 24:26], bst[:, 0:24]))
                v.wait_ge(sv.h, iv)
                v.tensor_copy(mu_cols[:, tt:tt + 1], bst[:, 24:25])
                sv.inc(v.tensor_scalar_add(bst[:, 25:26], bst[:, 25:26], 1e-5))

            @blk.scalar
            def _(s, tt=tt):
                sv.wait(s)
                isx = ss.inc(s.activation(bst[:, 25:26], bst[:, 25:26], ACT.Ln))
                s.wait_ge(ss.h, isx)
                ss.inc(s.activation(rsig_cols[:, tt:tt + 1], bst[:, 25:26],
                                    ACT.Exp, scale=-0.5))

            @blk.vector
            def _(v, tt=tt):
                ss.wait(v)
                sv.inc(v.tensor_scalar(xbs[tt % 2][:, :], xts[tt % 2][:, :],
                                       mu_cols[:, tt:tt + 1],
                                       rsig_cols[:, tt:tt + 1],
                                       ALU.subtract, ALU.mult))
            apply_val[tt] = sv.n
            if tt + 2 < NT:
                emit_load(tt + 2)

            for grp in range(4):
                @blk.tensor
                def _(t, tt=tt, grp=grp):
                    if grp == 0:
                        sv.wait(t, apply_val[tt])
                    for j in range(4):
                        c = grp * 4 + j
                        if (tt, c) >= (0, 4):
                            prev = ev_val.get((tt, c - 4),
                                              ev_val.get((tt - 1, c + 12), 0))
                            sv.wait(t, prev)
                        st.inc(t.transpose(tps[c % 4][:, :],
                                           xbs[tt % 2][:, 128 * c:128 * (c + 1)],
                                           ident[:, :]))
                        tr_val[(tt, c)] = st.n

                @blk.vector
                def _(v, tt=tt, grp=grp):
                    for j in range(4):
                        c = grp * 4 + j
                        st.wait(v, tr_val[(tt, c)])
                        sv.inc(v.tensor_copy(
                            bass.AP(xn_fm, c * D + tt * 128,
                                    [[NT * D, 128], [1, 128]]),
                            tps[c % 4][:, :]))
                        ev_val[(tt, c)] = sv.n

        def b2_tile(fs, sc):
            @blk.tensor
            def _(t, fs=fs, sc=sc):
                if sc == 0:
                    swS[fs % 2].wait(t, wload[fs])
                sv.wait(t, ev_val[(sc, 15)])
                idx = fs * NT + sc
                if idx >= 2:
                    sv.wait(t, evv[idx - 2])
                for e in range(NT):
                    mi = t.matmul(
                        ps[idx % 2][:, :],
                        bass.AP(xn_fm, e * D + sc * 128,
                                [[NT * D, 128], [1, 128]]),
                        ws[fs % 2][:, e * 512:(e + 1) * 512],
                        start=(e == 0), stop=(e == NT - 1))
                st.inc(mi)
            mmv[(fs, sc)] = st.n

            @blk.vector
            def _(v, fs=fs, sc=sc):
                idx = fs * NT + sc
                st.wait(v, mmv[(fs, sc)])
                sv.inc(v.tensor_copy(
                    q_tm[:, sc * D + fs * 512: sc * D + fs * 512 + 512],
                    ps[idx % 2][:, :]))
            evv[fs * NT + sc] = sv.n

        wslab_load(0, wq_h)
        wslab_load(1, wq_h)
        emit_load(0)
        emit_load(1)
        for tt in range(NT):
            b1_tile(tt)
            if tt >= 1:
                b2_tile(0, tt - 1)
        b2_tile(0, NT - 1)
        wslab_load(2, wq_h)
        for fs in range(1, NF):
            for sc in range(NT):
                b2_tile(fs, sc)
            if fs + 2 < NF:
                wslab_load(fs + 2, wq_h)

    if stop_after == "B2":
        return _dump_and_stop(nc, [("mu_cols", mu_cols), ("rsig_cols", rsig_cols),
                                   ("xn_fm", xn_fm), ("q_tm", q_tm)])

    sdG = [Sem(nc, "sdG0"), Sem(nc, "sdG1")]
    swGB = Sem(nc, "swGB")
    with ExitStack() as _ctx:
        _e = _ctx.enter_context
        w0 = _e(nc.sbuf_tensor("b25_w0", [128, NT * 512], BF16))
        w1 = _e(nc.sbuf_tensor("b25_w1", [128, NT * 512], BF16))
        g0 = _e(nc.sbuf_tensor("b25_g0", [128, 512], BF16))
        g1 = _e(nc.sbuf_tensor("b25_g1", [128, 512], BF16))
        p0 = _e(nc.psum_tensor("b25_p0", [128, 512], F32))
        p1 = _e(nc.psum_tensor("b25_p1", [128, 512], F32))
        blk = _e(nc.Block())
        ws = [w0, w1]
        gs = [g0, g1]
        ps = [p0, p1]
        wload = {}
        mmv = {}
        actv = {}
        spv = {}

        @blk.gpsimd
        def _(g):
            swGB.inc(g.dma_start(out=gate_b_bf[:, :], in_=gate_b_h[:, :]), 16)

        def wslab_load(fs):
            @blk.gpsimd
            def _(g, fs=fs):
                if fs >= 2:
                    st.wait(g, mmv[(fs - 2, NT - 1)])
                for c in range(NT):
                    swS[fs % 2].inc(g.dma_start(
                        out=ws[fs % 2][:, c * 512:(c + 1) * 512],
                        in_=gatew_h[c * 128:(c + 1) * 128,
                                 fs * 512:(fs + 1) * 512]), 16)
            wload[fs] = swS[fs % 2].n

        wslab_load(0)
        wslab_load(1)
        for fs in range(NF):
            for sc in range(NT):
                idx = fs * NT + sc

                @blk.tensor
                def _(t, fs=fs, sc=sc, idx=idx):
                    if idx == 0:
                        swGB.wait(t)
                    if sc == 0:
                        swS[fs % 2].wait(t, wload[fs])
                    if idx >= 2:
                        ss.wait(t, actv[idx - 2])
                    for e in range(NT):
                        t.matmul(
                            ps[idx % 2][:, :],
                            bass.AP(xn_fm, e * D + sc * 128,
                                    [[NT * D, 128], [1, 128]]),
                            ws[fs % 2][:, e * 512:(e + 1) * 512],
                            start=(e == 0), stop=False)
                    st.inc(t.matmul(ps[idx % 2][:, :], ones1[:, :],
                                    gate_b_bf[:, fs * 512:(fs + 1) * 512],
                                    start=False, stop=True))
                mmv[(fs, sc)] = st.n

                @blk.scalar
                def _(s, fs=fs, sc=sc, idx=idx):
                    st.wait(s, mmv[(fs, sc)])
                    if idx >= 2:
                        sdG[idx % 2].wait(s, spv[idx - 2])
                    ss.inc(s.activation(gs[idx % 2][:, :], ps[idx % 2][:, :],
                                        ACT.Sigmoid))
                actv[idx] = ss.n

                @blk.sync
                def _(sy, fs=fs, sc=sc, idx=idx):
                    ss.wait(sy, actv[idx])
                    sdG[idx % 2].inc(sy.dma_start(
                        out=gate_sp[sc * 128:(sc + 1) * 128,
                                    fs * 512:(fs + 1) * 512],
                        in_=gs[idx % 2][:, :]), 16)
                spv[idx] = sdG[idx % 2].n
            if fs + 2 < NF:
                wslab_load(fs + 2)

        @blk.sync
        def _(sy):
            sdG[0].wait(sy)
            sdG[1].wait(sy)

    if stop_after == "B25":
        return _dump_and_stop(nc, [], [("gate_sp", gate_sp)])

    sdW = [Sem(nc, "sdW0"), Sem(nc, "sdW1")]
    with ExitStack() as _ctx:
        _e = _ctx.enter_context
        w0 = _e(nc.sbuf_tensor("b3_w0", [128, NT * 512], BF16))
        w1 = _e(nc.sbuf_tensor("b3_w1", [128, NT * 512], BF16))
        kslab = _e(nc.sbuf_tensor("b3_k", [128, NT * 512], BF16))
        wt0 = _e(nc.sbuf_tensor("b3_wt0", [128, D], BF16))
        wt1 = _e(nc.sbuf_tensor("b3_wt1", [128, D], BF16))
        l_cols = _e(nc.sbuf_tensor("b3_l", [128, NT], F32))
        pk0 = _e(nc.psum_tensor("b3_pk0", [128, 512], F32))
        pk1 = _e(nc.psum_tensor("b3_pk1", [128, 512], F32))
        ps0 = _e(nc.psum_tensor("b3_ps0", [128, 512], F32))
        ps1 = _e(nc.psum_tensor("b3_ps1", [128, 512], F32))
        pl = _e(nc.psum_tensor("b3_pl", [128, NT], F32))
        blk = _e(nc.Block())
        ws = [w0, w1]
        pks = [pk0, pk1]
        pss = [ps0, ps1]
        wts = [wt0, wt1]
        wload = {}
        kmmv = {}
        kevv = {}
        smmv = {}
        expv = {}
        lmmv = {}
        spv = {}

        def wslab_load(fs):
            @blk.gpsimd
            def _(g, fs=fs):
                if fs >= 2:
                    st.wait(g, kmmv[(fs - 2, NT - 1)])
                for c in range(NT):
                    swS[fs % 2].inc(g.dma_start(
                        out=ws[fs % 2][:, c * 512:(c + 1) * 512],
                        in_=wk_h[c * 128:(c + 1) * 128,
                                 fs * 512:(fs + 1) * 512]), 16)
            wload[fs] = swS[fs % 2].n

        wslab_load(0)
        wslab_load(1)
        for fs in range(NF):
            for sc in range(NT):
                @blk.tensor
                def _(t, fs=fs, sc=sc):
                    if sc == 0:
                        swS[fs % 2].wait(t, wload[fs])
                    idx = fs * NT + sc
                    if idx >= 2:
                        sv.wait(t, kevv[idx - 2])
                    for e in range(NT):
                        mi = t.matmul(
                            pks[idx % 2][:, :],
                            bass.AP(xn_fm, e * D + sc * 128,
                                    [[NT * D, 128], [1, 128]]),
                            ws[fs % 2][:, e * 512:(e + 1) * 512],
                            start=(e == 0), stop=(e == NT - 1))
                    st.inc(mi)
                kmmv[(fs, sc)] = st.n

                @blk.vector
                def _(v, fs=fs, sc=sc):
                    idx = fs * NT + sc
                    st.wait(v, kmmv[(fs, sc)])
                    sv.inc(v.tensor_copy(kslab[:, sc * 512:(sc + 1) * 512],
                                         pks[idx % 2][:, :]))
                kevv[fs * NT + sc] = sv.n
            kslab_ready = sv.n
            if fs + 2 < NF:
                wslab_load(fs + 2)

            for fb in range(4):
                gfb = fs * 4 + fb
                for eN in range(NF):
                    gidx = gfb * NF + eN

                    @blk.tensor
                    def _(t, fs=fs, fb=fb, eN=eN, gidx=gidx,
                          kready=kslab_ready):
                        if fb == 0 and eN == 0:
                            sv.wait(t, kready)
                        if gidx >= 2:
                            ss.wait(t, expv.get(gidx - 2, 0))
                        for sc in range(NT):
                            mi = t.matmul(
                                pss[gidx % 2][:, :],
                                kslab[:, sc * 512 + fb * 128:
                                      sc * 512 + fb * 128 + 128],
                                q_tm[:, sc * D + eN * 512:
                                     sc * D + eN * 512 + 512],
                                start=(sc == 0), stop=(sc == NT - 1))
                        st.inc(mi)
                    smmv[gidx] = st.n

                    @blk.scalar
                    def _(s, gfb=gfb, eN=eN, gidx=gidx):
                        st.wait(s, smmv[gidx])
                        if gfb >= 2:
                            st.wait(s, lmmv.get((gfb - 2) * NF + eN, 0))
                            sdW[gfb % 2].wait(s, spv.get(gfb - 2, 0))
                        ss.inc(s.activation(
                            wts[gfb % 2][:, eN * 512:(eN + 1) * 512],
                            pss[gidx % 2][:, :], ACT.Exp, scale=0.125))
                    expv[gidx] = ss.n

                    @blk.tensor
                    def _(t, gfb=gfb, eN=eN, gidx=gidx):
                        ss.wait(t, expv[gidx])
                        for cc in range(4):
                            ec = eN * 4 + cc
                            mi = t.matmul(
                                pl[:, ec:ec + 1],
                                wts[gfb % 2][:, ec * 128:(ec + 1) * 128],
                                ones_col[:, :],
                                start=(gfb == 0 and ec == 0),
                                stop=(gfb == NT - 1 and ec == NT - 1))
                        st.inc(mi)
                    lmmv[gidx] = st.n

                @blk.sync
                def _(sy, gfb=gfb):
                    ss.wait(sy, expv[gfb * NF + NF - 1])
                    sdW[gfb % 2].inc(sy.dma_start(out=wT_sp[:, gfb, :],
                                                  in_=wts[gfb % 2][:, :]), 16)
                spv[gfb] = sdW[gfb % 2].n

        @blk.vector
        def _(v):
            st.wait(v)
            iv = sv.inc(v.tensor_copy(l_cols[:, :], pl[:, :]))
            v.wait_ge(sv.h, iv)
            sv.inc(v.reciprocal(r_cols[:, :], l_cols[:, :]))

        @blk.sync
        def _(sy):
            sdW[0].wait(sy)
            sdW[1].wait(sy)

    q_ctx.__exit__(None, None, None)
    if stop_after == "B3":
        return _dump_and_stop(nc, [("r_cols", r_cols)], [("wT_sp", wT_sp)])

    swV = [Sem(nc, "swV0"), Sem(nc, "swV1")]
    v_ctx = nc.sbuf_tensor("v_fm", [128, NT * D], BF16, side="right")
    v_fm = v_ctx.__enter__()
    with ExitStack() as _ctx:
        _e = _ctx.enter_context
        w0 = _e(nc.sbuf_tensor("b35_w0", [128, NT * 128], BF16))
        w1 = _e(nc.sbuf_tensor("b35_w1", [128, NT * 128], BF16))
        p0 = _e(nc.psum_tensor("b35_p0", [128, 512], F32))
        p1 = _e(nc.psum_tensor("b35_p1", [128, 512], F32))
        blk = _e(nc.Block())
        ws = [w0, w1]
        ps = [p0, p1]
        wload = {}
        mmv = {}
        evv = {}

        def wcol_load(fb):
            @blk.gpsimd
            def _(g, fb=fb):
                if fb >= 2:
                    st.wait(g, mmv[(fb - 2, NF - 1)])
                for c in range(NT):
                    swV[fb % 2].inc(g.dma_start(
                        out=ws[fb % 2][:, c * 128:(c + 1) * 128],
                        in_=wv_h[fb, c * 128:(c + 1) * 128, :]), 16)
            wload[fb] = swV[fb % 2].n

        wcol_load(0)
        wcol_load(1)
        for fb in range(NT):
            for sN in range(NF):
                @blk.tensor
                def _(t, fb=fb, sN=sN):
                    if sN == 0:
                        swV[fb % 2].wait(t, wload[fb])
                    idx = fb * NF + sN
                    if idx >= 2:
                        sv.wait(t, evv[idx - 2])
                    for e in range(NT):
                        mi = t.matmul(
                            ps[idx % 2][:, :],
                            ws[fb % 2][:, e * 128:(e + 1) * 128],
                            bass.AP(xn_fm, e * D + sN * 512,
                                    [[NT * D, 128], [1, 512]]),
                            start=(e == 0), stop=(e == NT - 1))
                    st.inc(mi)
                mmv[(fb, sN)] = st.n

                @blk.vector
                def _(v, fb=fb, sN=sN):
                    idx = fb * NF + sN
                    st.wait(v, mmv[(fb, sN)])
                    sv.inc(v.tensor_copy(
                        v_fm[:, fb * D + sN * 512: fb * D + sN * 512 + 512],
                        ps[idx % 2][:, :]))
                evv[fb * NF + sN] = sv.n
            if fb + 2 < NT:
                wcol_load(fb + 2)

    xn_ctx.__exit__(None, None, None)
    if stop_after == "B35":
        return _dump_and_stop(nc, [("v_fm", v_fm)])

    swO = Sem(nc, "swO")
    sdC = [Sem(nc, "sdC0"), Sem(nc, "sdC1")]
    sdA = [Sem(nc, "sdA0"), Sem(nc, "sdA1")]
    wo_ctx = nc.sbuf_tensor("wo_bf", [128, NT * D], BF16)
    wo_bf = wo_ctx.__enter__()
    with ExitStack() as _ctx:
        _e = _ctx.enter_context
        c0 = _e(nc.sbuf_tensor("b4_c0", [128, NT * 128], BF16))
        c1 = _e(nc.sbuf_tensor("b4_c1", [128, NT * 128], BF16))
        a0 = _e(nc.sbuf_tensor("b4_a0", [128, D], BF16))
        a1 = _e(nc.sbuf_tensor("b4_a1", [128, D], BF16))
        pA = _e(nc.psum_tensor("b4_pA", [128, D], F32))
        pB = _e(nc.psum_tensor("b4_pB", [128, D], F32))
        blk = _e(nc.Block())
        cs = [c0, c1]
        asb = [a0, a1]
        pp = [pA, pB]
        cload = {}
        mmv = {}
        evv = {}
        spv = {}

        @blk.gpsimd
        def _(g):
            for c in range(NT):
                swO.inc(g.dma_start(
                    out=wo_bf[:, c * D:(c + 1) * D],
                    in_=wo_h[c * 128:(c + 1) * 128, :]), 16)
        wo_loaded = swO.n

        def col_load(ec):
            @blk.sync
            def _(sy, ec=ec):
                if ec >= 2:
                    st.wait(sy, mmv[ec - 2])
                sdC[ec % 2].inc(sy.dma_start(
                    out=cs[ec % 2][:, :].rearrange("p (c f) -> p c f", f=128),
                    in_=wT_sp[:, :, ec * 128:(ec + 1) * 128]), 16)
            cload[ec] = sdC[ec % 2].n

        col_load(0)
        col_load(1)
        for ec in range(NT):
            @blk.tensor
            def _(t, ec=ec):
                sdC[ec % 2].wait(t, cload[ec])
                if ec >= 2:
                    sv.wait(t, evv[ec - 2])
                for sN in range(NF):
                    for fb in range(NT):
                        mi = t.matmul(
                            pp[ec % 2][:, sN * 512:(sN + 1) * 512],
                            cs[ec % 2][:, fb * 128:(fb + 1) * 128],
                            v_fm[:, fb * D + sN * 512: fb * D + sN * 512 + 512],
                            start=(fb == 0), stop=(fb == NT - 1))
                    st.inc(mi)
            mmv[ec] = st.n

            @blk.vector
            def _(v, ec=ec):
                st.wait(v, mmv[ec])
                if ec >= 2:
                    sdA[ec % 2].wait(v, spv[ec - 2])
                sv.inc(v.tensor_scalar_mul(asb[ec % 2][:, :], pp[ec % 2][:, :],
                                           r_cols[:, ec:ec + 1]))
            evv[ec] = sv.n

            @blk.sync
            def _(sy, ec=ec):
                sv.wait(sy, evv[ec])
                sdA[ec % 2].inc(sy.dma_start(out=attn_sp[:, ec, :],
                                             in_=asb[ec % 2][:, :]), 16)
            spv[ec] = sdA[ec % 2].n
            if ec + 2 < NT:
                col_load(ec + 2)

        @blk.sync
        def _(sy):
            sdA[0].wait(sy)
            sdA[1].wait(sy)

    v_ctx.__exit__(None, None, None)
    if stop_after == "B4":
        return _dump_and_stop(nc, [], [("attn_sp", attn_sp)])

    sdL = [Sem(nc, "sdL0"), Sem(nc, "sdL1")]
    sdO = [Sem(nc, "sdO0"), Sem(nc, "sdO1")]
    sdOG = Sem(nc, "sdOG")
    with ExitStack() as _ctx:
        _e = _ctx.enter_context
        og = _e(nc.sbuf_tensor("b6_og", [128, D], F32))
        ob = _e(nc.sbuf_tensor("b6_ob", [128, D], F32))
        x0 = _e(nc.sbuf_tensor("b6_x0", [128, D], F32))
        x1 = _e(nc.sbuf_tensor("b6_x1", [128, D], F32))
        g0 = _e(nc.sbuf_tensor("b6_g0", [128, D], BF16))
        g1 = _e(nc.sbuf_tensor("b6_g1", [128, D], BF16))
        ac0 = _e(nc.sbuf_tensor("b6_ac0", [128, NT * 128], BF16))
        ac1 = _e(nc.sbuf_tensor("b6_ac1", [128, NT * 128], BF16))
        wk1 = _e(nc.sbuf_tensor("b6_w1", [128, D], F32))
        wk2 = _e(nc.sbuf_tensor("b6_w2", [128, D], F32))
        o0 = _e(nc.sbuf_tensor("b6_o0", [128, D], F32))
        o1 = _e(nc.sbuf_tensor("b6_o1", [128, D], F32))
        bst = _e(nc.sbuf_tensor("b6_st", [128, 28], F32))
        pA = _e(nc.psum_tensor("b6_pA", [128, D], F32))
        pB = _e(nc.psum_tensor("b6_pB", [128, D], F32))
        blk = _e(nc.Block())
        xs = [x0, x1]
        gs = [g0, g1]
        acs = [ac0, ac1]
        os_ = [o0, o1]
        pp = [pA, pB]
        lval = {}
        mmv = {}
        combv = {}
        outv = {}

        @blk.sync
        def _(sy):
            sdOG.inc(sy.dma_start(out=og[:, :], in_=og128_h[:, :]), 16)
            sdOG.inc(sy.dma_start(out=ob[:, :], in_=ob128_h[:, :]), 16)

        def loads(sc):
            @blk.sync
            def _(sy, sc=sc):
                if sc >= 2:
                    sv.wait(sy, combv[sc - 2])
                sdL[sc % 2].inc(sy.dma_start(
                    out=xs[sc % 2][:, :],
                    in_=x_h[sc * 128:(sc + 1) * 128, :]), 16)
                sdL[sc % 2].inc(sy.dma_start(
                    out=gs[sc % 2][:, :],
                    in_=gate_sp[sc * 128:(sc + 1) * 128, :]), 16)
                sdL[sc % 2].inc(sy.dma_start(
                    out=acs[sc % 2][:, :].rearrange("p (c f) -> p c f", f=128),
                    in_=attn_sp[:, :, sc * 128:(sc + 1) * 128]), 16)
            lval[sc] = sdL[sc % 2].n

        loads(0)
        loads(1)
        for sc in range(NT):
            @blk.tensor
            def _(t, sc=sc):
                if sc == 0:
                    swO.wait(t, wo_loaded)
                sdL[sc % 2].wait(t, lval[sc])
                if sc >= 2:
                    sv.wait(t, combv[sc - 2])
                for gN in range(NF):
                    for ec in range(NT):
                        mi = t.matmul(
                            pp[sc % 2][:, gN * 512:(gN + 1) * 512],
                            acs[sc % 2][:, ec * 128:(ec + 1) * 128],
                            wo_bf[:, ec * D + gN * 512: ec * D + gN * 512 + 512],
                            start=(ec == 0), stop=(ec == NT - 1))
                    st.inc(mi)
            mmv[sc] = st.n

            @blk.scalar
            def _(s, sc=sc):
                sdL[sc % 2].wait(s, lval[sc])
                if sc >= 2:
                    sv.wait(s, combv[sc - 2])
                ss.inc(s.copy(wk2[:, :], gs[sc % 2][:, :]))

            @blk.vector
            def _(v, sc=sc):
                sdL[sc % 2].wait(v, lval[sc])
                sv.wait(v)
                if sc == 0:
                    sdOG.wait(v)
                iv = sv.inc(v.tensor_scalar(wk1[:, :], xs[sc % 2][:, :],
                                            mu_cols[:, sc:sc + 1],
                                            rsig_cols[:, sc:sc + 1],
                                            ALU.subtract, ALU.mult))
                ss.wait(v)
                iv = sv.inc(v.tensor_mul(wk2[:, :], wk2[:, :], bc_rep[:, :]))
                v.wait_ge(sv.h, iv)
                iv = sv.inc(v.tensor_add(wk1[:, :], wk1[:, :], wk2[:, :]))
                v.wait_ge(sv.h, iv)
                for c in range(NF):
                    ist = v.bn_stats(bst[:, 6 * c:6 * c + 6],
                                     wk1[:, 512 * c:512 * (c + 1)])
                iv = sv.inc(ist)
                v.wait_ge(sv.h, iv)
                iv = sv.inc(v.bn_aggr(bst[:, 24:26], bst[:, 0:24]))
                v.wait_ge(sv.h, iv)
                sv.inc(v.tensor_scalar_add(bst[:, 25:26], bst[:, 25:26], 1e-6))

            @blk.scalar
            def _(s, sc=sc):
                sv.wait(s)
                isx = ss.inc(s.activation(bst[:, 25:26], bst[:, 25:26], ACT.Ln))
                s.wait_ge(ss.h, isx)
                ss.inc(s.activation(bst[:, 26:27], bst[:, 25:26], ACT.Exp,
                                    scale=-0.5))

            @blk.vector
            def _(v, sc=sc):
                ss.wait(v)
                iv = sv.inc(v.tensor_scalar(wk1[:, :], wk1[:, :], bst[:, 24:25],
                                            bst[:, 26:27], ALU.subtract,
                                            ALU.mult))
                v.wait_ge(sv.h, iv)
                iv = sv.inc(v.tensor_mul(wk1[:, :], wk1[:, :], og[:, :]))
                v.wait_ge(sv.h, iv)
                iv = sv.inc(v.tensor_add(wk1[:, :], wk1[:, :], ob[:, :]))
                st.wait(v, mmv[sc])
                if sc >= 2:
                    sdO[sc % 2].wait(v, outv[sc - 2])
                v.wait_ge(sv.h, iv)
                iv = sv.inc(v.tensor_add(os_[sc % 2][:, :], xs[sc % 2][:, :],
                                         pp[sc % 2][:, :]))
                v.wait_ge(sv.h, iv)
                sv.inc(v.tensor_add(os_[sc % 2][:, :], os_[sc % 2][:, :],
                                    wk1[:, :]))
            combv[sc] = sv.n

            @blk.sync
            def _(sy, sc=sc):
                sv.wait(sy, combv[sc])
                sdO[sc % 2].inc(sy.dma_start(
                    out=out_h[sc * 128:(sc + 1) * 128, :],
                    in_=os_[sc % 2][:, :]), 16)
            outv[sc] = sdO[sc % 2].n
            if sc + 2 < NT:
                loads(sc + 2)

        @blk.sync
        def _(sy):
            sdO[0].wait(sy)
            sdO[1].wait(sy)

    wo_ctx.__exit__(None, None, None)
    return nc


_CACHE = {}


def kernel(L0_attn_x, L0_attn_Wq, L0_attn_Wk, L0_attn_Wv, L0_attn_Wo,
           L0_ffn_in_gamma, L0_ffn_in_beta, L0_ffn_agg_w, L0_ffn_agg_b,
           L0_ffn_pn_gamma, L0_ffn_pn_beta, L0_ffn_d1_w, L0_ffn_d1_b,
           L0_ffn_d2_w, L0_ffn_d2_b, L0_ffn_bc_w, L0_ffn_bc_b,
           L0_ffn_gate_w, L0_ffn_gate_b, L0_ffn_out_gamma, L0_ffn_out_beta):
    f32 = np.float32
    X = np.ascontiguousarray(np.asarray(L0_attn_x, dtype=f32))
    wq = np.ascontiguousarray(np.asarray(L0_attn_Wq, dtype=f32))
    wk = np.ascontiguousarray(np.asarray(L0_attn_Wk, dtype=f32))
    wv = np.ascontiguousarray(np.asarray(L0_attn_Wv, dtype=f32))
    wo = np.ascontiguousarray(np.asarray(L0_attn_Wo, dtype=f32))
    gatew = np.ascontiguousarray(np.asarray(L0_ffn_gate_w, dtype=f32))
    aggw = np.ascontiguousarray(np.asarray(L0_ffn_agg_w, dtype=f32))
    bcw = np.ascontiguousarray(np.asarray(L0_ffn_bc_w, dtype=f32))
    d1w = np.ascontiguousarray(np.asarray(L0_ffn_d1_w, dtype=f32))
    d2w = np.ascontiguousarray(np.asarray(L0_ffn_d2_w, dtype=f32))

    wv_blk = np.ascontiguousarray(wv.reshape(D, NT, 128).transpose(1, 0, 2))
    lastrows = np.ascontiguousarray(X[:, D - 1, :])

    def rep8(vec, w=D):
        return np.ascontiguousarray(
            np.broadcast_to(np.asarray(vec, dtype=f32).reshape(1, w), (B, w)))

    in_g8 = rep8(L0_ffn_in_gamma)
    in_b8 = rep8(L0_ffn_in_beta)
    pn_g8 = rep8(L0_ffn_pn_gamma)
    pn_b8 = rep8(L0_ffn_pn_beta)
    agg_b8 = rep8(L0_ffn_agg_b)
    d2_b8 = rep8(L0_ffn_d2_b)
    bc_b8 = rep8(L0_ffn_bc_b)
    og128 = np.ascontiguousarray(np.broadcast_to(
        np.asarray(L0_ffn_out_gamma, dtype=f32).reshape(1, D), (128, D)))
    ob128 = np.ascontiguousarray(np.broadcast_to(
        np.asarray(L0_ffn_out_beta, dtype=f32).reshape(1, D), (128, D)))
    gate_b_row = np.ascontiguousarray(
        np.asarray(L0_ffn_gate_b, dtype=f32).reshape(1, D))
    d1b = np.asarray(L0_ffn_d1_b, dtype=f32)

    ident = np.eye(128, dtype=BFD)
    ones_col = np.ones((128, 1), dtype=BFD)
    ones1 = np.ones((1, 128), dtype=BFD)

    if "nc" not in _CACHE:
        _CACHE["nc"] = build_nc()
    nc = _CACHE["nc"]

    in_maps = []
    for b in range(B):
        selb = np.zeros((B, 128), dtype=f32)
        selb[b, :] = 1.0
        in_maps.append({
            "x": X[b],
            "wq": wq, "wk": wk, "wo": wo, "gatew": gatew,
            "aggw_sl": np.ascontiguousarray(aggw[:, b * 256:(b + 1) * 256]),
            "bcw_sl": np.ascontiguousarray(bcw[:, b * 256:(b + 1) * 256]),
            "wv_blk": wv_blk,
            "d1s": np.ascontiguousarray(d1w[:, b * HSL:(b + 1) * HSL]),
            "d2s": np.ascontiguousarray(d2w[b * HSL:(b + 1) * HSL, :]),
            "lastrows": lastrows,
            "in_g8": in_g8, "in_b8": in_b8, "pn_g8": pn_g8, "pn_b8": pn_b8,
            "agg_b8": agg_b8,
            "d1_b8": np.ascontiguousarray(np.broadcast_to(
                d1b[b * HSL:(b + 1) * HSL].reshape(1, HSL), (B, HSL))),
            "d2_b8": d2_b8, "bc_b8": bc_b8,
            "og128": og128, "ob128": ob128, "gate_b_row": gate_b_row,
            "sel_rep": selb, "ident": ident, "ones_col": ones_col,
            "ones1": ones1,
        })

    res = run_bass_kernel_spmd(nc, in_maps, core_ids=list(range(B)))
    out = np.stack([res.results[b]["out"] for b in range(B)], axis=0)
    return out
